# version 11 (final)
# speedup vs baseline: 1.1467x; 1.1258x over previous
import contextlib
import numpy as np
import ml_dtypes

import concourse.bass as bass
import concourse.mybir as mybir
from concourse.bass_utils import run_bass_kernel_spmd

F32 = mybir.dt.float32
BF16 = mybir.dt.bfloat16
AF = mybir.ActivationFunctionType
ALU = mybir.AluOpType
AX = mybir.AxisListType

BF = ml_dtypes.bfloat16

B, L, D = 2, 16384, 2048
H, DKH = 16, 128
NCORE, NSEG = 8, 4
T = L // NSEG
HALO = 128
XR = T + HALO
MAC, NMAC = 512, T // 512
CH, CPM, NCH = 128, 4, T // 128
DCH = D // 128
E, NE, INTER, ICH = 8, 9, 1024, 8
EPS = 1e-5
NEGBIG = -60000.0


def configure(T_):
    global T, XR, NMAC, NCH
    T = T_
    XR = T + HALO
    NMAC = T // MAC
    NCH = T // CH

ENGINES = ("sync", "scalar", "vector", "tensor", "gpsimd")


class Res:
    __slots__ = ("w", "r", "name")

    def __init__(self, name=""):
        self.w = None
        self.r = []
        self.name = name


class Op:
    __slots__ = ("eng", "fn", "dma", "deps", "dependents", "val", "inc", "idx", "slot")

    def __init__(self, eng, fn, dma):
        self.eng = eng
        self.fn = fn
        self.dma = dma
        self.deps = []
        self.dependents = []
        self.val = None
        self.inc = False
        self.idx = 0
        self.slot = 0


NSLOT = 8


class Sched:
    def __init__(self, nc, stack):
        self.nc = nc
        self.ops = {e: [] for e in ENGINES}
        self.all_ops = []
        self.fence = []
        self.last = {e: None for e in ENGINES}
        self.last_dma = {}
        self.dma_count = {e: 0 for e in ("sync", "scalar", "vector", "gpsimd")}
        self.csem = {e: stack.enter_context(nc.semaphore(f"c_{e}")) for e in ENGINES}
        self.dsem = {e: [stack.enter_context(nc.semaphore(f"d_{e}{i}")) for i in range(NSLOT)]
                     for e in ("sync", "scalar", "vector", "gpsimd")}

    def op(self, eng, fn, reads=(), writes=(), dma=False, extra=(), exempt=False):
        o = Op(eng, fn, dma)
        deps = set()
        for r in reads:
            if r.w is not None:
                deps.add(r.w)
        for wres in writes:
            if wres.w is not None:
                deps.add(wres.w)
            deps.update(wres.r)
        deps.update(extra)
        deps.update(self.fence)
        if dma:
            n = self.dma_count[eng]
            o.slot = n % NSLOT
            self.dma_count[eng] = n + 1
            prev = self.last_dma.get((eng, o.slot))
            if prev is not None:
                deps.add(prev)
        deps.discard(o)
        o.deps = list(deps)
        for d in o.deps:
            d.dependents.append(o)
        for r in reads:
            r.r.append(o)
        for wres in writes:
            wres.w = o
            wres.r = []
        self.ops[eng].append(o)
        self.all_ops.append(o)
        if dma:
            self.last_dma[(eng, o.slot)] = o
        if not exempt:
            self.last[eng] = o
        return o

    def barrier(self):
        deps = set(x for x in self.last.values() if x is not None)
        deps |= set(self.last_dma.values())
        newfence = []
        for e in ENGINES:
            o = self.op(e, lambda eng: eng.nop(), extra=deps)
            newfence.append(o)
        self.fence = newfence

    def finalize(self):
        deps = set(x for x in self.last.values() if x is not None)
        deps |= set(self.last_dma.values())
        self.op("gpsimd", lambda eng: eng.nop(), extra=deps)

        for e in ENGINES:
            cval = 0
            dval = {}
            for o in self.ops[e]:
                if o.dma:
                    o.inc = True
                    dval[o.slot] = dval.get(o.slot, 0) + 16
                    o.val = dval[o.slot]
                else:
                    need = False
                    for dep_user in o.dependents:
                        if o.eng == "tensor" and dep_user.eng == "tensor":
                            continue
                        need = True
                        break
                    if need:
                        o.inc = True
                        cval += 1
                        o.val = cval

        with self.nc.Block() as block:
            def emit_engine(eng_name):
                def run(eng):
                    hw = {}
                    for o in self.ops[eng_name]:
                        for d in o.deps:
                            if d.eng == "tensor" and o.eng == "tensor" and not d.dma:
                                continue
                            if not d.inc:
                                continue
                            key = (d.eng, d.dma, d.slot if d.dma else 0)
                            if hw.get(key, 0) >= d.val:
                                continue
                            hw[key] = d.val
                            sem = self.dsem[d.eng][d.slot] if d.dma else self.csem[d.eng]
                            eng.wait_ge(sem, d.val)
                        inst = o.fn(eng)
                        if o.inc:
                            if o.dma:
                                inst.then_inc(self.dsem[o.eng][o.slot], 16)
                            else:
                                inst.then_inc(self.csem[o.eng], 1)
                return run

            block.sync(emit_engine("sync"))
            block.scalar(emit_engine("scalar"))
            block.vector(emit_engine("vector"))
            block.tensor(emit_engine("tensor"))
            block.gpsimd(emit_engine("gpsimd"))


def _fn(f, *args, **kw):
    return lambda eng: f(eng, *args, **kw)


def build_nc(dbg=False):
    nc = bass.Bass("TRN2", target_bir_lowering=False)
    top = contextlib.ExitStack()

    def din(name, shape, dt=F32):
        return nc.dram_tensor(name, list(shape), dt, kind="ExternalInput").ap()

    def dint(name, shape, dt, debug_out=False):
        kind = "ExternalOutput" if (dbg and debug_out) else "Internal"
        return nc.dram_tensor(name, list(shape), dt, kind=kind).ap()

    x_in = din("x", (XR, D))
    Wq_in = din("Wq", (D, D)); Wk_in = din("Wk", (D, D)); Wv_in = din("Wv", (D, D))
    Wg_in = din("Wg", (D, D)); Wo_in = din("Wo", (D, D))
    Wb_in = din("Wb", (D, H)); bb_in = din("bb", (H,))
    qcw_in = din("qcw", (D, 4)); qcb_in = din("qcb", (D,))
    kcw_in = din("kcw", (D, 4)); kcb_in = din("kcb", (D,))
    qnw_in = din("qnw", (DKH,)); qnb_in = din("qnb", (DKH,))
    knw_in = din("knw", (DKH,)); knb_in = din("knb", (DKH,))
    gw_in = din("gate_w", (E, D))
    w1_in = din("w1", (E, INTER, D)); w2_in = din("w2", (E, D, INTER)); w3_in = din("w3", (E, INTER, D))
    ws1_in = din("ws1", (INTER, D)); ws2_in = din("ws2", (D, INTER)); ws3_in = din("ws3", (INTER, D))
    idb_in = din("c_id_bf", (128, 128), BF16)
    idf_in = din("c_id_f", (128, 128))
    ones_in = din("c_ones", (1, 128))
    mb_in = din("c_maskbig", (128, 128))
    selc_in = din("c_selcol", (128, 16 * H), BF16)
    rows_in = din("c_rowsel", (16, 128 * H))
    rowsb_in = din("c_rowsel_bf", (16, 128 * H), BF16)
    rs8_in = din("c_rowsel8", (8, 8 * 8))
    rs8w_in = din("c_rowsel8w", (8, 128 * 8))
    sel8_in = din("c_sel8", (8, 1))
    msk8_in = din("c_mask8", (8, 8))
    y_out = nc.dram_tensor("y", [T, D], F32, kind="ExternalOutput").ap()

    xln_d = dint("xln_d", (DCH, 128, XR), BF16, True)
    qh_d = dint("qh_d", (H, 128, T), BF16, True)
    qa_d = dint("qa_d", (H, 128, T), BF16, True)
    qd_d = dint("qd_d", (H, 128, T), BF16, True)
    kh_d = dint("kh_d", (H, 128, T), BF16, True)
    gf_d = dint("gf_d", (H, 128, T), BF16, True)
    ol_d = dint("ol_d", (H, 128, T), BF16, True)
    xmid_d = dint("xmid_d", (T, D), F32, True)
    xln2_d = dint("xln2_d", (DCH, 128, T), BF16, True)
    crow_d = dint("crow_d", (E, T), F32, True)
    g3_d = dint("g3_d", (NE, NCH, 128, ICH, 128), BF16, True)
    w1b_d = dint("w1b_d", (NE, INTER, D), BF16)
    w3b_d = dint("w3b_d", (NE, INTER, D), BF16)
    w2b_d = dint("w2b_d", (NE, D, INTER), BF16)
    agin_d = dint("agin_d", (2176, 128), F32)
    agout_d = dint("agout_d", (8 * 2176, 128), F32)

    sk = Sched(nc, top)

    _names = {}

    def _uniq(name):
        n = _names.get(name, 0)
        _names[name] = n + 1
        return name if n == 0 else f"{name}__{n}"

    def sb(name, shape, dt=F32, stack=None):
        return (stack or top).enter_context(nc.sbuf_tensor(_uniq(name), list(shape), dt))

    def ps(name, shape, dt=F32, stack=None):
        return (stack or top).enter_context(nc.psum_tensor(_uniq(name), list(shape), dt))

    rowsel8w = sb("rowsel8w", (8, 128 * 8)); R_rs8w = Res("rs8w")
    ast = contextlib.ExitStack()
    _sb0 = sb
    sb = lambda name, shape, dt=F32, stack=None: _sb0(name, shape, dt, stack or ast)
    id_bf = sb("id_bf", (128, 128), BF16); R_idb = Res("idb")
    id_f = sb("id_f", (128, 128)); R_idf = Res("idf")
    ones_f = sb("ones_f", (1, 128)); R_ones = Res("ones")
    maskbig = sb("maskbig", (128, 128)); R_mb = Res("mb")
    selcol = sb("selcol", (128, 16 * H), BF16); R_selc = Res("selc")
    rowsel = sb("rowsel", (16, 128 * H)); R_rows = Res("rows")
    rowselb = sb("rowselb", (16, 128 * H), BF16); R_rowsb = Res("rowsb")
    rowsel8 = sb("rowsel8", (8, 64)); R_rs8 = Res("rs8")
    sel8 = sb("sel8", (8, 1)); R_sel8 = Res("sel8")
    mask8 = sb("mask8", (8, 8)); R_msk8 = Res("msk8")
    lgl = sb("lgl", (16, T))
    bfm = sb("bfmb", (16, T))
    R_lg = Res("lglob"); R_bf = Res("bfm")
    zer = sb("zer", (16, MAC)); R_zer = Res("zer")
    qnw = sb("s_qnw", (128, 1)); qnb = sb("s_qnb", (128, 1))
    knw = sb("s_knw", (128, 1)); knb = sb("s_knb", (128, 1))
    bbc = sb("s_bbc", (16, 1))
    qcw = sb("s_qcw", (128, 64)); qcb = sb("s_qcb", (128, 16))
    kcw = sb("s_kcw", (128, 64)); kcb = sb("s_kcb", (128, 16))
    R_smw = Res("smallw")

    O = sk.op

    def ld(dst_ap, src_ap, res, eng="sync", slow=False):
        O(eng, _fn(lambda e, o_, i_, s_: e.dma_start(out=o_, in_=i_, allow_slow_non_contiguous=s_),
                   dst_ap, src_ap, slow),
          writes=[res], dma=True)

    ld(id_bf[:, :], idb_in, R_idb)
    ld(id_f[:, :], idf_in, R_idf)
    ld(ones_f[:, :], ones_in, R_ones)
    ld(maskbig[:, :], mb_in, R_mb)
    ld(selcol[:, :], selc_in, R_selc)
    ld(rowsel[:, :], rows_in, R_rows)
    ld(rowselb[:, :], rowsb_in, R_rowsb)
    ld(rowsel8[:, :], rs8_in, R_rs8)
    ld(rowsel8w[:, :], rs8w_in, R_rs8w)
    ld(sel8[:, :], sel8_in, R_sel8)
    ld(mask8[:, :], msk8_in, R_msk8)
    ld(qnw[:, :], qnw_in.rearrange("(p o) -> p o", o=1), R_smw)
    ld(qnb[:, :], qnb_in.rearrange("(p o) -> p o", o=1), R_smw)
    ld(knw[:, :], knw_in.rearrange("(p o) -> p o", o=1), R_smw)
    ld(knb[:, :], knb_in.rearrange("(p o) -> p o", o=1), R_smw)
    ld(bbc[:, :], bb_in.rearrange("(p o) -> p o", o=1), R_smw)
    ld(qcw[:, :].rearrange("p (c j) -> p c j", j=4), qcw_in.rearrange("(c p) j -> p c j", p=128), R_smw)
    ld(qcb[:, :], qcb_in.rearrange("(c p) -> p c", p=128), R_smw, slow=True)
    ld(kcw[:, :].rearrange("p (c j) -> p c j", j=4), kcw_in.rearrange("(c p) j -> p c j", p=128), R_smw)
    ld(kcb[:, :], kcb_in.rearrange("(c p) -> p c", p=128), R_smw, slow=True)
    O("vector", _fn(lambda e, a: e.memset(a, 0.0), zer[:, :]), writes=[R_zer])
    cst = sb("cstb", (128, 2)); R_cst = Res("cst")
    O("vector", _fn(lambda e, a: e.memset(a, float(EPS)), cst[:, 0:1]), writes=[R_cst])
    O("vector", _fn(lambda e, a: e.memset(a, 1.0), cst[:, 1:2]), writes=[R_cst])

    R_w1b = [Res(f"w1b{e}") for e in range(NE)]
    R_w3b = [Res(f"w3b{e}") for e in range(NE)]
    R_w2b = [Res(f"w2b{e}") for e in range(NE)]

    def emit_w0():
        for e in range(E):
            O("gpsimd", _fn(lambda g, o_, i_: g.dma_start(out=o_, in_=i_), w1b_d[e], w1_in[e]),
              writes=[R_w1b[e]], dma=True, exempt=True)
            O("gpsimd", _fn(lambda g, o_, i_: g.dma_start(out=o_, in_=i_), w3b_d[e], w3_in[e]),
              writes=[R_w3b[e]], dma=True, exempt=True)
            O("gpsimd", _fn(lambda g, o_, i_: g.dma_start(out=o_, in_=i_), w2b_d[e], w2_in[e]),
              writes=[R_w2b[e]], dma=True, exempt=True)
        O("gpsimd", _fn(lambda g: g.dma_start(out=w1b_d[E], in_=ws1_in)), writes=[R_w1b[E]], dma=True, exempt=True)
        O("gpsimd", _fn(lambda g: g.dma_start(out=w3b_d[E], in_=ws3_in)), writes=[R_w3b[E]], dma=True, exempt=True)
        O("gpsimd", _fn(lambda g: g.dma_start(out=w2b_d[E], in_=ws2_in)), writes=[R_w2b[E]], dma=True, exempt=True)

    def load_big_weight(wsb, res_list, w_ap, stack_eng="gpsimd"):
        for c0 in range(0, DCH, 4):
            src = w_ap.rearrange("(c p) d -> p c d", p=128)[:, c0:c0 + 4, :]
            dst = wsb[:, c0 * D:(c0 + 4) * D].rearrange("p (c d) -> p c d", d=D)
            O(stack_eng, _fn(lambda e, o_, i_: e.dma_start(out=o_, in_=i_), dst, src),
              writes=[res_list[c0 // 4]], dma=True)

    def mm(psum_ap, lhsT_ap, rhs_ap, reads, wres, start, stop):
        return O("tensor",
                 _fn(lambda e, o_, l_, r_, s0, s1: e.matmul(o_, l_, r_, start=s0, stop=s1),
                     psum_ap, lhsT_ap, rhs_ap, start, stop),
                 reads=reads, writes=[wres] if stop or start else [wres])

    def ln_norm(stack, x_res, x_ap, out_ap, out_res, stats, R_stats, junk_ap, R_junk, dim):
        s_sum, s_sumsq, s_m, s_var, s_rstd = (stats[:, i:i + 1] for i in range(5))
        O("vector", _fn(lambda e, o_, i_: e.tensor_reduce(o_, i_, AX.X, ALU.add), s_sum, x_ap),
          reads=[x_res], writes=[R_stats])
        O("scalar", _fn(lambda e, o_, i_, a_: e.activation(o_, i_, AF.Square, accum_out=a_),
                        junk_ap, x_ap, s_sumsq),
          reads=[x_res], writes=[R_junk, R_stats])
        O("vector", _fn(lambda e, o_, i_: e.tensor_scalar(o_, i_, 1.0 / dim, None, ALU.mult),
                        s_m, s_sum), reads=[R_stats], writes=[R_stats])
        O("scalar", _fn(lambda e, o_, i_: e.activation(o_, i_, AF.Square), s_var, s_m),
          reads=[R_stats], writes=[R_stats])
        O("vector", _fn(lambda e, o_, i_, m_: e.scalar_tensor_tensor(o_, i_, 1.0 / dim, m_, ALU.mult, ALU.subtract),
                        s_var, s_sumsq, s_var), reads=[R_stats], writes=[R_stats])
        O("scalar", _fn(lambda e, o_, i_, b_: e.activation(o_, i_, AF.Ln, bias=b_), s_var, s_var, cst[:, 0:1]),
          reads=[R_stats, R_cst], writes=[R_stats])
        O("scalar", _fn(lambda e, o_, i_: e.activation(o_, i_, AF.Exp, scale=-0.5), s_rstd, s_var),
          reads=[R_stats], writes=[R_stats])
        O("vector", _fn(lambda e, o_, i_, m_, r_: e.tensor_scalar(o_, i_, m_, r_, ALU.subtract, ALU.mult),
                        out_ap, x_ap, s_m, s_rstd),
          reads=[x_res, R_stats], writes=[out_res])

    def phase_a1_qk(which):
        st = contextlib.ExitStack()
        isq = which == "q"
        Wsb = sb("Wqk", (128, DCH * D), BF16, st)
        R_W = [Res(f"Wqk{i}") for i in range(4)]
        load_big_weight(Wsb, R_W, Wq_in if isq else Wk_in)
        if isq:
            Wbsb = sb("Wbsb", (128, DCH * H), BF16, st)
            R_Wb_ = [Res(f"Wb{i}") for i in range(2)]
            for c0 in range(0, DCH, 8):
                src = Wb_in.rearrange("(c p) h -> p c h", p=128)[:, c0:c0 + 8, :]
                dst = Wbsb[:, c0 * H:(c0 + 8) * H].rearrange("p (c h) -> p c h", h=H)
                O("gpsimd", _fn(lambda e, o_, i_: e.dma_start(out=o_, in_=i_), dst, src),
                  writes=[R_Wb_[c0 // 8]], dma=True)

        if isq:
            emit_w0()
        cw, cb = (qcw, qcb) if isq else (kcw, kcb)
        nwc, nbc = (qnw, qnb) if isq else (knw, knb)
        dst_h = qh_d if isq else kh_d

        xs = [sb("xs0", (128, D), F32, st)]
        R_xs = [Res("xs0")]
        xlt_ = [sb(f"xlt{i}", (128, D), BF16, st) for i in range(2)]
        R_xlt_ = [Res(f"xlt{i}") for i in range(2)]
        xfm = sb("xfm", (128, DCH * MAC), BF16, st); R_xfm = Res("xfm")
        qext = sb("qext", (128, H * (MAC + 3)), BF16, st)
        R_qext = [Res(f"qext{h}") for h in range(H)]
        qsil = sb("qsil", (128, H * MAC), BF16, st)
        R_qsil = [Res(f"qsil{h}") for h in range(H)]
        q2 = [sb(f"q2_{i}", (128, MAC), BF16, st) for i in range(2)]
        R_q2 = [Res(f"q2_{i}") for i in range(2)]
        ctmp = [sb(f"ctmp{i}", (128, MAC), BF16, st) for i in range(4)]
        R_ct = [Res(f"ctmp{i}") for i in range(4)]
        mnegb = sb("mnegb", (16, MAC), BF16, st)
        rstdb = sb("rstdb", (16, MAC), BF16, st)
        R_stp = Res("statp")
        varb = sb("varb", (16, MAC), F32, st); R_varb = Res("varb")
        stg = [sb(f"stg{i}", (128, MAC), BF16, st) for i in range(6)]
        R_stg = [Res(f"stg{i}") for i in range(6)]
        lns_ = [sb(f"lns{i}", (128, 8), F32, st) for i in range(2)]
        R_lns_ = [Res(f"lns{i}") for i in range(2)]
        if isq:
            zl0 = sb("zl0", (16, MAC), F32, st)
            zlb = sb("zlb", (16, MAC), F32, st)
            R_z = Res("z"); R_la = Res("la")
            arelb_ = [sb(f"arelb{i}", (16, MAC), BF16, st) for i in range(2)]
            gexpb_ = [sb(f"gexpb{i}", (16, MAC), BF16, st) for i in range(2)]
            R_agx_ = [Res(f"agx{i}") for i in range(2)]

        pP = [ps(f"pP{i}", (128, 512), F32, st) for i in range(2)]
        R_pP = [Res(f"pP{i}") for i in range(2)]
        pT = [ps(f"pT{i}", (128, 1024), BF16, st) for i in range(2)]
        R_pT = [Res(f"pT{i}") for i in range(2)]
        pS1 = ps("pS1", (128, 512), F32, st); R_pS1 = Res("pS1")
        pS2 = ps("pS2", (128, 512), F32, st); R_pS2 = Res("pS2")
        pB = [ps(f"pB{i}", (128, 512), F32, st) for i in range(2)]
        R_pB = [Res(f"pB{i}") for i in range(2)]

        def xln_load_macro(m):
            off = HALO + m * MAC if m >= 0 else 0
            width = MAC if m >= 0 else HALO
            src = xln_d.rearrange("d p t -> p d t")[:, :, off:off + width]
            dst = xfm[:, 0:DCH * width].rearrange("p (d t) -> p d t", t=width)
            O("sync", _fn(lambda e, o_, i_: e.dma_start(out=o_, in_=i_), dst, src),
              writes=[R_xfm], dma=True)

        def ln1_and_transpose(m):
            nsub = 1 if m < 0 else CPM
            base = 0 if m < 0 else HALO + m * MAC
            for s in range(nsub):
                r0 = base + s * 128
                xb = xs[0]; R_xb = R_xs[0]
                xlt = xlt_[s % 2]; R_xlt = R_xlt_[s % 2]
                lns = lns_[s % 2]; R_lns = R_lns_[s % 2]
                O("sync", _fn(lambda e, o_, i_: e.dma_start(out=o_, in_=i_),
                              xb[:, :], x_in[r0:r0 + 128, :]),
                  writes=[R_xb], dma=True)
                ln_norm(st, R_xb, xb[:, :], xlt[:, :], R_xlt, lns, R_lns, xlt[:, :], R_xlt, D)
                for d in range(DCH):
                    pt = pT[d % 2]; R_pt = R_pT[d % 2]
                    O("tensor", _fn(lambda e, o_, i_, id_: e.transpose(o_, i_, id_),
                                    pt[:, 0:128], xlt[:, d * 128:(d + 1) * 128], id_bf[:, :]),
                      reads=[R_xlt, R_idb], writes=[R_pt])
                    col = d * (128 if m < 0 else MAC) + (s * 128 if m >= 0 else 0)
                    eng = "vector" if d % 2 == 0 else "scalar"
                    fncp = (lambda e, o_, i_: e.tensor_copy(o_, i_)) if eng == "vector" else \
                           (lambda e, o_, i_: e.copy(o_, i_))
                    O(eng, _fn(fncp, xfm[:, col:col + 128], pt[:, 0:128]),
                      reads=[R_pt], writes=[R_xfm])
            width = 128 if m < 0 else MAC
            off = 0 if m < 0 else HALO + m * MAC
            src = xfm[:, 0:DCH * width].rearrange("p (d t) -> p d t", t=width)
            dsta = xln_d.rearrange("d p t -> p d t")[:, :, off:off + width]
            O("scalar", _fn(lambda e, o_, i_: e.dma_start(out=o_, in_=i_), dsta, src),
              reads=[R_xfm], dma=True)

        def proj_macro(m, width):
            for c in range(H):
                pp = pP[c % 2]; R_pp = R_pP[c % 2]
                for k in range(DCH):
                    mm(pp[:, 0:width], Wsb[:, k * D + c * 128: k * D + (c + 1) * 128],
                       xfm[:, k * width:(k + 1) * width],
                       reads=[R_W[k // 4], R_xfm], wres=R_pp, start=(k == 0), stop=(k == DCH - 1))
                if m < 0:
                    O("scalar", _fn(lambda e, o_, i_: e.copy(o_, i_),
                                    qext[:, c * (MAC + 3): c * (MAC + 3) + 3],
                                    pp[:, width - 3:width]),
                      reads=[R_pp], writes=[R_qext[c]])
                else:
                    O("scalar", _fn(lambda e, o_, i_: e.copy(o_, i_),
                                    qext[:, c * (MAC + 3) + 3: c * (MAC + 3) + 3 + width],
                                    pp[:, 0:width]),
                      reads=[R_pp], writes=[R_qext[c]])

        def conv_silu(m):
            for c in range(H):
                h0 = c * (MAC + 3)
                i0, i1 = 2 * (c % 2), 2 * (c % 2) + 1
                ct0, ct1 = ctmp[i0], ctmp[i1]
                Rc0, Rc1 = R_ct[i0], R_ct[i1]
                O("vector", _fn(lambda e, o_, i_, w_: e.tensor_scalar(o_, i_, w_, None, ALU.mult),
                          ct0[:, :], qext[:, h0:h0 + MAC], cw[:, c * 4:c * 4 + 1]),
                  reads=[R_qext[c], R_smw], writes=[Rc0])
                O("vector", _fn(lambda e, o_, i_, w_, a_: e.scalar_tensor_tensor(o_, i_, w_, a_, ALU.mult, ALU.add),
                          ct1[:, :], qext[:, h0 + 1:h0 + 1 + MAC], cw[:, c * 4 + 1:c * 4 + 2], ct0[:, :]),
                  reads=[R_qext[c], R_smw, Rc0], writes=[Rc1])
                O("vector", _fn(lambda e, o_, i_, w_, a_: e.scalar_tensor_tensor(o_, i_, w_, a_, ALU.mult, ALU.add),
                          ct0[:, :], qext[:, h0 + 2:h0 + 2 + MAC], cw[:, c * 4 + 2:c * 4 + 3], ct1[:, :]),
                  reads=[R_qext[c], R_smw, Rc1], writes=[Rc0])
                O("vector", _fn(lambda e, o_, i_, w_, a_: e.scalar_tensor_tensor(o_, i_, w_, a_, ALU.mult, ALU.add),
                          ct1[:, :], qext[:, h0 + 3:h0 + 3 + MAC], cw[:, c * 4 + 3:c * 4 + 4], ct0[:, :]),
                  reads=[R_qext[c], R_smw, Rc0], writes=[Rc1])
                O("scalar", _fn(lambda e, o_, i_, b_: e.activation(o_, i_, AF.Silu, bias=b_),
                                qsil[:, c * MAC:(c + 1) * MAC], ct1[:, :], cb[:, c:c + 1]),
                  reads=[Rc1, R_smw], writes=[R_qsil[c]])
                O("vector", _fn(lambda e, o_, i_: e.tensor_copy(o_, i_),
                                qext[:, h0:h0 + 3], qext[:, h0 + MAC:h0 + MAC + 3]),
                  reads=[R_qext[c]], writes=[R_qext[c]])

        def headln_stats(m):
            off = m * MAC
            for c in range(H):
                q2b = q2[c % 2]; R_q2b = R_q2[c % 2]
                O("scalar", _fn(lambda e, o_, i_: e.activation(o_, i_, AF.Square),
                                q2b[:, :], qsil[:, c * MAC:(c + 1) * MAC]),
                  reads=[R_qsil[c]], writes=[R_q2b])
                mm(pS1[0:16, 0:MAC], selcol[:, c * 16:(c + 1) * 16], qsil[:, c * MAC:(c + 1) * MAC],
                   reads=[R_selc, R_qsil[c]], wres=R_pS1, start=(c == 0), stop=(c == H - 1))
                mm(pS2[0:16, 0:MAC], selcol[:, c * 16:(c + 1) * 16], q2b[:, :],
                   reads=[R_selc, R_q2b], wres=R_pS2, start=(c == 0), stop=(c == H - 1))
            O("vector", _fn(lambda e, o_, i_: e.tensor_scalar(o_, i_, -1.0 / DKH, None, ALU.mult),
                            mnegb[:, :], pS1[0:16, 0:MAC]), reads=[R_pS1], writes=[R_stp])
            O("scalar", _fn(lambda e, o_, i_: e.activation(o_, i_, AF.Square),
                            varb[:, :], mnegb[:, :]), reads=[R_stp], writes=[R_varb])
            O("vector", _fn(lambda e, o_, i_, m_: e.scalar_tensor_tensor(o_, i_, 1.0 / DKH, m_, ALU.mult, ALU.subtract),
                            varb[:, :], pS2[0:16, 0:MAC], varb[:, :]),
              reads=[R_pS2, R_varb], writes=[R_varb])
            O("scalar", _fn(lambda e, o_, i_, b_: e.activation(o_, i_, AF.Ln, bias=b_),
                            varb[:, :], varb[:, :], cst[0:16, 0:1]), reads=[R_varb, R_cst], writes=[R_varb])
            O("scalar", _fn(lambda e, o_, i_: e.activation(o_, i_, AF.Exp, scale=-0.5),
                            rstdb[:, :], varb[:, :]), reads=[R_varb], writes=[R_stp])

        def headln_apply(m):
            off = m * MAC
            if isq:
                arelb, gexpb, R_agx = arelb_[m % 2], gexpb_[m % 2], R_agx_[m % 2]
            for c in range(H):
                par = (c % 2) * 3
                s0, s1, s2 = stg[par], stg[par + 1], stg[par + 2]
                Rs0, Rs1, Rs2 = R_stg[par], R_stg[par + 1], R_stg[par + 2]
                i0, i1 = 2 * (c % 2), 2 * (c % 2) + 1
                ct0, ct1 = ctmp[i0], ctmp[i1]
                Rc0, Rc1 = R_ct[i0], R_ct[i1]
                if c % 2 == 0:
                    pb0, pb1, Rb0, Rb1 = pB[0], pB[1], R_pB[0], R_pB[1]
                else:
                    pb0, pb1, Rb0, Rb1 = pS1, pS2, R_pS1, R_pS2
                mm(pb0[:, 0:MAC], rowselb[:, c * 128:(c + 1) * 128], mnegb[:, :],
                   reads=[R_rowsb, R_stp], wres=Rb0, start=True, stop=True)
                mm(pb1[:, 0:MAC], rowselb[:, c * 128:(c + 1) * 128], rstdb[:, :],
                   reads=[R_rowsb, R_stp], wres=Rb1, start=True, stop=True)
                O("vector", _fn(lambda e, o_, a_, b_: e.tensor_tensor(o_, a_, b_, ALU.add),
                                ct0[:, :], qsil[:, c * MAC:(c + 1) * MAC], pb0[:, 0:MAC]),
                  reads=[R_qsil[c], Rb0], writes=[Rc0])
                O("vector", _fn(lambda e, o_, a_, b_: e.tensor_tensor(o_, a_, b_, ALU.mult),
                                ct1[:, :], ct0[:, :], pb1[:, 0:MAC]),
                  reads=[Rc0, Rb1], writes=[Rc1])
                O("vector", _fn(lambda e, o_, i_, w_, b_: e.tensor_scalar(o_, i_, w_, b_, ALU.mult, ALU.add),
                                s0[:, :], ct1[:, :], nwc[:, :], nbc[:, :]),
                  reads=[Rc1, R_smw], writes=[Rs0])
                O("sync", _fn(lambda e, o_, i_: e.dma_start(out=o_, in_=i_),
                              dst_h.rearrange("d p t -> p d t")[:, c, off:off + MAC], s0[:, :]),
                  reads=[Rs0], dma=True)
                if isq:
                    mm(pb0[:, 0:MAC], rowselb[:, c * 128:(c + 1) * 128], arelb[:, :],
                       reads=[R_rowsb, R_agx], wres=Rb0, start=True, stop=True)
                    O("vector", _fn(lambda e, o_, a_, b_: e.tensor_tensor(o_, a_, b_, ALU.mult),
                                    s1[:, :], s0[:, :], pb0[:, 0:MAC]),
                      reads=[Rs0, Rb0], writes=[Rs1])
                    O("scalar", _fn(lambda e, o_, i_: e.dma_start(out=o_, in_=i_),
                                    qa_d.rearrange("d p t -> p d t")[:, c, off:off + MAC], s1[:, :]),
                      reads=[Rs1], dma=True)
                    mm(pb1[:, 0:MAC], rowselb[:, c * 128:(c + 1) * 128], gexpb[:, :],
                       reads=[R_rowsb, R_agx], wres=Rb1, start=True, stop=True)
                    O("vector", _fn(lambda e, o_, a_, b_: e.tensor_tensor(o_, a_, b_, ALU.mult),
                                    s2[:, :], s0[:, :], pb1[:, 0:MAC]),
                      reads=[Rs0, Rb1], writes=[Rs2])
                    O("sync", _fn(lambda e, o_, i_: e.dma_start(out=o_, in_=i_),
                                  qd_d.rearrange("d p t -> p d t")[:, c, off:off + MAC], s2[:, :]),
                      reads=[Rs2], dma=True)

        def decay_macro(m):
            off = m * MAC
            arelb, gexpb, R_agx = arelb_[m % 2], gexpb_[m % 2], R_agx_[m % 2]
            for k in range(DCH):
                mm(pS1[0:16, 0:MAC], Wbsb[:, k * H:(k + 1) * H], xfm[:, k * MAC:(k + 1) * MAC],
                   reads=[R_Wb_[k // 8], R_xfm], wres=R_pS1, start=(k == 0), stop=(k == DCH - 1))
            O("vector", _fn(lambda e, o_, i_, b_: e.tensor_scalar(o_, i_, b_, None, ALU.add),
                            zl0[:, :], pS1[0:16, 0:MAC], bbc[:, :]),
              reads=[R_pS1, R_smw], writes=[R_z])
            O("scalar", _fn(lambda e, o_, i_: e.activation(o_, i_, AF.Sigmoid),
                            bfm[:, off:off + MAC], zl0[:, :]),
              reads=[R_z], writes=[R_bf])
            O("scalar", _fn(lambda e, o_, i_: e.activation(o_, i_, AF.Abs),
                            zlb[:, :], zl0[:, :]), reads=[R_z], writes=[R_la])
            O("scalar", _fn(lambda e, o_, i_: e.activation(o_, i_, AF.Exp, scale=-1.0),
                            zlb[:, :], zlb[:, :]), reads=[R_la], writes=[R_la])
            O("scalar", _fn(lambda e, o_, i_, b_: e.activation(o_, i_, AF.Ln, bias=b_),
                            zlb[:, :], zlb[:, :], cst[0:16, 1:2]), reads=[R_la, R_cst], writes=[R_la])
            O("vector", _fn(lambda e, o_, i_: e.tensor_scalar(o_, i_, 0.0, None, ALU.max),
                            zl0[:, :], zl0[:, :]), reads=[R_z], writes=[R_z])
            O("vector", _fn(lambda e, o_, i_, a_: e.scalar_tensor_tensor(o_, i_, -1.0, a_, ALU.mult, ALU.subtract),
                            zl0[:, :], zl0[:, :], zlb[:, :]),
              reads=[R_z, R_la], writes=[R_z])
            init = zer[:, 0:1] if m == 0 else lgl[:, off - 1:off]
            O("vector", _fn(lambda e, o_, d0, d1, in_: e.tensor_tensor_scan(o_, d0, d1, in_, ALU.add, ALU.add),
                            lgl[:, off:off + MAC], zl0[:, :], zer[:, :], init),
              reads=[R_z, R_zer, R_lg], writes=[R_lg])
            for c in range(CPM):
                o2 = off + c * CH
                bc = zer[:, 0:1] if o2 == 0 else lgl[:, o2 - 1:o2]
                O("vector", _fn(lambda e, o_, i_, s_: e.tensor_scalar(o_, i_, s_, None, ALU.subtract),
                                zlb[:, c * CH:(c + 1) * CH], lgl[:, o2:o2 + CH], bc),
                  reads=[R_lg], writes=[R_la])
            O("scalar", _fn(lambda e, o_, i_: e.activation(o_, i_, AF.Exp),
                            arelb[:, :], zlb[:, :]), reads=[R_la], writes=[R_agx])
            O("scalar", _fn(lambda e, o_, i_: e.activation(o_, i_, AF.Exp),
                            gexpb[:, :], lgl[:, off:off + MAC]), reads=[R_lg], writes=[R_agx])

        def frontend(m):
            if m >= NMAC:
                return
            if isq:
                ln1_and_transpose(m)
                decay_macro(m)
            else:
                xln_load_macro(m)
            proj_macro(m, MAC)

        if isq:
            ln1_and_transpose(-1)
        else:
            xln_load_macro(-1)
        proj_macro(-1, HALO)
        frontend(0)
        for m in range(NMAC):
            conv_silu(m)
            headln_stats(m)
            frontend(m + 1)
            headln_apply(m)
        sk.barrier()
        st.close()

    def phase_a1g():
        st = contextlib.ExitStack()
        Wsb = sb("Wgsb", (128, DCH * D), BF16, st)
        R_W = [Res(f"Wg{i}") for i in range(4)]
        load_big_weight(Wsb, R_W, Wg_in)
        xfm = sb("xfmg", (128, DCH * MAC), BF16, st); R_xfm = Res("xfmg")
        stg = [sb(f"gst{i}", (128, MAC), BF16, st) for i in range(2)]
        R_stg = [Res(f"gst{i}") for i in range(2)]
        pP = [ps(f"pgP{i}", (128, 512), F32, st) for i in range(2)]
        R_pP = [Res(f"pgP{i}") for i in range(2)]
        for m in range(NMAC):
            off = HALO + m * MAC
            src = xln_d.rearrange("d p t -> p d t")[:, :, off:off + MAC]
            dstp = xfm[:, :].rearrange("p (d t) -> p d t", t=MAC)
            O("sync", _fn(lambda e, o_, i_: e.dma_start(out=o_, in_=i_), dstp, src),
              writes=[R_xfm], dma=True)
            for c in range(H):
                pp = pP[c % 2]; R_pp = R_pP[c % 2]
                for k in range(DCH):
                    mm(pp[:, 0:MAC], Wsb[:, k * D + c * 128: k * D + (c + 1) * 128],
                       xfm[:, k * MAC:(k + 1) * MAC],
                       reads=[R_W[k // 4], R_xfm], wres=R_pp, start=(k == 0), stop=(k == DCH - 1))
                sg = stg[c % 2]; R_sg = R_stg[c % 2]
                O("scalar", _fn(lambda e, o_, i_: e.activation(o_, i_, AF.Silu), sg[:, :], pp[:, 0:MAC]),
                  reads=[R_pp], writes=[R_sg])
                O("sync", _fn(lambda e, o_, i_: e.dma_start(out=o_, in_=i_),
                              gf_d.rearrange("d p t -> p d t")[:, c, m * MAC:(m + 1) * MAC], sg[:, :]),
                  reads=[R_sg], dma=True)
        sk.barrier()
        st.close()

    def phase_a2():
        st = contextlib.ExitStack()
        Wsb = sb("Wvsb", (128, DCH * D), BF16, st)
        R_W = [Res(f"Wv{i}") for i in range(4)]
        load_big_weight(Wsb, R_W, Wv_in)
        xfm = sb("xfm2", (128, DCH * MAC), BF16, st); R_xfm = Res("xfm2")
        HW2 = 2 * CH
        qm = sb("qm", (128, H * HW2), BF16, st); R_qm = Res("qm")
        qam = sb("qam", (128, H * HW2), BF16, st); R_qam = Res("qam")
        km = sb("km", (128, H * HW2), BF16, st); R_km = Res("km")
        vtk = [sb(f"vtk{i}", (128, D), BF16, st) for i in range(2)]
        R_vtk = [Res(f"vtk{i}") for i in range(2)]
        wch = sb("wch", (16, CH), F32, st); R_wch = Res("wch")
        ac16 = sb("ac16", (16, 1), F32, st); R_ac16 = Res("ac16")
        scur = sb("scur", (128, H * 128), F32, st)
        salt = sb("salt", (128, H * 128), F32, st)
        R_s = [[Res(f"s{h}_{i}") for i in range(2)] for h in range(H)]
        sbf = sb("sbf", (128, H * 128), BF16, st)
        R_sbf = [Res(f"sbf{h}") for h in range(H)]
        lrelb = sb("lrelb", (16, CH), F32, st)
        neglb = sb("neglb", (16, CH), F32, st)
        R_lrl = Res("lrl")
        bwt = sb("bwt", (128, 32), F32, st); R_bwt = Res("bwt")
        acr = sb("acr", (1, 16), F32, st); R_acr = Res("acr")
        ab = sb("ab", (128, 16), F32, st); R_ab = Res("ab")
        dxp = [sb(f"dxp{i}", (128, 128), F32, st) for i in range(2)]
        R_dxp = [Res(f"dxp{i}") for i in range(2)]
        msb = [sb(f"msb{i}", (128, 128), BF16, st) for i in range(2)]
        R_msb = [Res(f"msb{i}") for i in range(2)]
        kw = [sb(f"kw{i}", (128, 128), BF16, st) for i in range(2)]
        R_kw = [Res(f"kw{i}") for i in range(2)]
        ost = [sb(f"ost{i}", (128, D), BF16, st) for i in range(2)]
        R_ost = [Res(f"ost{i}") for i in range(2)]

        pV = ps("pV", (128, 512), F32, st); R_pV = Res("pV")
        pM = ps("pM", (128, 512), F32, st); R_pM = Res("pM")
        pGD = [ps(f"pGD{i}", (128, 512), F32, st) for i in range(2)]
        R_pGD = [Res(f"pGD{i}") for i in range(2)]
        pOS = [ps(f"pOS{i}", (128, 512), F32, st) for i in range(2)]
        R_pOS = [Res(f"pOS{i}") for i in range(2)]
        pKp = [ps(f"pKp{i}", (128, 1024), BF16, st) for i in range(2)]
        R_pKp = [Res(f"pKp{i}") for i in range(2)]

        for h in range(H):
            O("vector", _fn(lambda e, a: e.memset(a, 0.0), scur[:, h * 128:(h + 1) * 128]),
              writes=[R_s[h][0]])
            O("gpsimd", _fn(lambda e, a: e.memset(a, 0.0), sbf[:, h * 128:(h + 1) * 128]),
              writes=[R_sbf[h]])

        for m in range(NMAC):
            off = m * MAC
            src = xln_d.rearrange("d p t -> p d t")[:, :, HALO + off:HALO + off + MAC]
            O("sync", _fn(lambda e, o_, i_: e.dma_start(out=o_, in_=i_),
                          xfm[:, :].rearrange("p (d t) -> p d t", t=MAC), src),
              writes=[R_xfm], dma=True)
            for c in range(CPM):
                if c % 2 == 0:
                    hoff = off + c * CH
                    for buf, res, dsrc in ((qm, R_qm, qh_d), (qam, R_qam, qa_d), (km, R_km, kh_d)):
                        O("scalar" if dsrc is qh_d else "sync",
                          _fn(lambda e, o_, i_: e.dma_start(out=o_, in_=i_),
                              buf[:, :].rearrange("p (d t) -> p d t", t=HW2),
                              dsrc.rearrange("d p t -> p d t")[:, :, hoff:hoff + HW2]),
                          writes=[res], dma=True)
                ci = m * CPM + c
                o2 = off + c * CH
                vt = vtk[ci % 2]; R_vt = R_vtk[ci % 2]
                for vtile in range(4):
                    for k in range(DCH):
                        mm(pV[:, 0:512],
                           xfm[:, k * MAC + c * CH: k * MAC + (c + 1) * CH],
                           Wsb[:, k * D + vtile * 512: k * D + (vtile + 1) * 512],
                           reads=[R_xfm, R_W[k // 4]], wres=R_pV, start=(k == 0), stop=(k == DCH - 1))
                    O("scalar", _fn(lambda e, o_, i_: e.copy(o_, i_),
                                    vt[:, vtile * 512:(vtile + 1) * 512], pV[:, 0:512]),
                      reads=[R_pV], writes=[R_vt])
                bc = zer[:, 0:1] if o2 == 0 else lgl[:, o2 - 1:o2]
                O("vector", _fn(lambda e, o_, i_, s_: e.tensor_scalar(o_, i_, s_, None, ALU.subtract),
                                lrelb[:, :], lgl[:, o2:o2 + CH], bc),
                  reads=[R_lg], writes=[R_lrl])
                O("vector", _fn(lambda e, o_, i_: e.tensor_scalar(o_, i_, -1.0, None, ALU.mult),
                                neglb[:, :], lgl[:, o2:o2 + CH]),
                  reads=[R_lg], writes=[R_lrl])
                O("vector", _fn(lambda e, o_, i_, s_: e.tensor_scalar(o_, i_, s_, None, ALU.subtract),
                                wch[:, :], lrelb[:, :], lrelb[:, CH - 1:CH]),
                  reads=[R_lrl], writes=[R_wch])
                O("scalar", _fn(lambda e, o_, i_: e.activation(o_, i_, AF.Exp, scale=-1.0),
                                wch[:, :], wch[:, :]), reads=[R_wch], writes=[R_wch])
                O("vector", _fn(lambda e, o_, a_, b_: e.tensor_tensor(o_, a_, b_, ALU.mult),
                                wch[:, :], wch[:, :], bfm[:, o2:o2 + CH]),
                  reads=[R_wch, R_bf], writes=[R_wch])
                O("tensor", _fn(lambda e, o_, i_, id_: e.transpose(o_, i_, id_),
                                pM[:, 16:32], bfm[0:16, o2:o2 + CH], id_f[0:16, 0:16]),
                  reads=[R_bf, R_idf], writes=[R_pM])
                O("vector", _fn(lambda e, o_, i_: e.tensor_copy(o_, i_), bwt[:, 0:16], pM[:, 16:32]),
                  reads=[R_pM], writes=[R_bwt])
                O("tensor", _fn(lambda e, o_, i_, id_: e.transpose(o_, i_, id_),
                                pM[:, 32:48], wch[:, :], id_f[0:16, 0:16]),
                  reads=[R_wch, R_idf], writes=[R_pM])
                O("vector", _fn(lambda e, o_, i_: e.tensor_copy(o_, i_), bwt[:, 16:32], pM[:, 32:48]),
                  reads=[R_pM], writes=[R_bwt])
                O("scalar", _fn(lambda e, o_, i_: e.activation(o_, i_, AF.Exp),
                                ac16[:, :], lrelb[:, CH - 1:CH]),
                  reads=[R_lrl], writes=[R_ac16])
                O("tensor", _fn(lambda e, o_, i_, id_: e.transpose(o_, i_, id_),
                                pM[0:1, 128:144], ac16[:, :], id_f[0:16, 0:16]),
                  reads=[R_ac16, R_idf], writes=[R_pM])
                O("scalar", _fn(lambda e, o_, i_: e.copy(o_, i_), acr[:, :], pM[0:1, 128:144]),
                  reads=[R_pM], writes=[R_acr])
                mm(pM[:, 0:16], ones_f[:, :], acr[:, :], reads=[R_ones, R_acr], wres=R_pM,
                   start=True, stop=True)
                O("vector", _fn(lambda e, o_, i_: e.tensor_copy(o_, i_), ab[:, :], pM[:, 0:16]),
                  reads=[R_pM], writes=[R_ab])

                c2 = c % 2
                for h in range(H):
                    hp = h % 2
                    qs = qm[:, h * HW2 + c2 * CH: h * HW2 + (c2 + 1) * CH]
                    qas = qam[:, h * HW2 + c2 * CH: h * HW2 + (c2 + 1) * CH]
                    ks = km[:, h * HW2 + c2 * CH: h * HW2 + (c2 + 1) * CH]
                    vs = vt[:, h * 128:(h + 1) * 128]
                    s_par = ci % 2
                    s_old = (scur if s_par == 0 else salt)[:, h * 128:(h + 1) * 128]
                    s_new = (salt if s_par == 0 else scur)[:, h * 128:(h + 1) * 128]
                    R_sold, R_snew = R_s[h][s_par], R_s[h][1 - s_par]
                    gd, Rgd = pGD[hp], R_pGD[hp]
                    osb, Ros = pOS[hp], R_pOS[hp]
                    kb, Rkb = pKp[hp], R_pKp[hp]
                    mm(gd[:, 0:128], ks, qs, reads=[R_km, R_qm], wres=Rgd, start=True, stop=True)
                    mm(gd[:, 128:256], rowsel[:, h * 128:(h + 1) * 128], lgl[:, o2:o2 + CH],
                       reads=[R_rows, R_lg], wres=Rgd, start=True, stop=False)
                    mm(gd[:, 128:256], neglb[:, :], rowsel[:, h * 128:(h + 1) * 128],
                       reads=[R_lrl, R_rows], wres=Rgd, start=False, stop=True)
                    O("vector", _fn(lambda e, o_, a_, b_: e.tensor_tensor(o_, a_, b_, ALU.add),
                                    dxp[hp][:, :], gd[:, 128:256], maskbig[:, :]),
                      reads=[Rgd, R_mb], writes=[R_dxp[hp]])
                    O("scalar", _fn(lambda e, o_, i_: e.activation(o_, i_, AF.Exp),
                                    dxp[hp][:, :], dxp[hp][:, :]), reads=[R_dxp[hp]], writes=[R_dxp[hp]])
                    O("vector", _fn(lambda e, o_, i_, s_, b_: e.scalar_tensor_tensor(o_, i_, s_, b_, ALU.mult, ALU.mult),
                                    msb[hp][:, :], gd[:, 0:128], bwt[:, h:h + 1], dxp[hp][:, :]),
                      reads=[Rgd, R_bwt, R_dxp[hp]], writes=[R_msb[hp]])
                    mm(osb[:, 0:128], vs, msb[hp][:, :], reads=[R_vt, R_msb[hp]], wres=Ros,
                       start=True, stop=False)
                    mm(osb[:, 0:128], sbf[:, h * 128:(h + 1) * 128], qas,
                       reads=[R_sbf[h], R_qam], wres=Ros, start=False, stop=True)
                    O("scalar", _fn(lambda e, o_, i_: e.copy(o_, i_),
                                    ost[c % 2][:, h * 128:(h + 1) * 128], osb[:, 0:128]),
                      reads=[Ros], writes=[R_ost[c % 2]])
                    O("tensor", _fn(lambda e, o_, i_, id_: e.transpose(o_, i_, id_),
                                    kb[:, 0:128], ks, id_bf[:, :]),
                      reads=[R_km, R_idb], writes=[Rkb])
                    O("scalar", _fn(lambda e, o_, i_, w_: e.activation(o_, i_, AF.Copy, scale=w_),
                                    kw[hp][:, :], kb[:, 0:128], bwt[:, 16 + h:17 + h]),
                      reads=[Rkb, R_bwt], writes=[R_kw[hp]])
                    mm(osb[:, 128:256], kw[hp][:, :], vs, reads=[R_kw[hp], R_vt], wres=Ros,
                       start=True, stop=True)
                    O("vector", _fn(lambda e, o_, i_, s_, a_: e.scalar_tensor_tensor(o_, i_, s_, a_, ALU.mult, ALU.add),
                                    s_new, s_old, ab[:, h:h + 1], osb[:, 128:256]),
                      reads=[R_sold, R_ab, Ros], writes=[R_snew])
                    O("vector", _fn(lambda e, o_, i_: e.tensor_copy(o_, i_),
                                    sbf[:, h * 128:(h + 1) * 128], s_new),
                      reads=[R_snew], writes=[R_sbf[h]])
                O("gpsimd", _fn(lambda e, o_, i_: e.dma_start(out=o_, in_=i_),
                                ol_d.rearrange("d p t -> p d t")[:, :, o2:o2 + CH],
                                ost[c % 2][:, :].rearrange("p (d t) -> p d t", t=CH)),
                  reads=[R_ost[c % 2]], dma=True)

        R_agin = Res("agin")
        for zi in range(2):
            O("sync", _fn(lambda e, o_, i_: e.dma_start(out=o_, in_=i_),
                          agin_d[2048 + zi * 64:2048 + (zi + 1) * 64, :], zer[:, :]),
              reads=[R_zer], writes=[R_agin], dma=True)
        par = NCH % 2
        sfin = scur if par == 0 else salt
        for h in range(H):
            O("sync", _fn(lambda e, o_, i_: e.dma_start(out=o_, in_=i_),
                          agin_d[h * 128:(h + 1) * 128, :], sfin[:, h * 128:(h + 1) * 128]),
              reads=[R_s[h][par]], dma=True)
        O("scalar", _fn(lambda e, o_, i_: e.activation(o_, i_, AF.Exp),
                        ac16[:, :], lgl[:, T - 1:T]), reads=[R_lg], writes=[R_ac16])
        O("tensor", _fn(lambda e, o_, i_, id_: e.transpose(o_, i_, id_),
                        pM[0:1, 128:144], ac16[:, :], id_f[0:16, 0:16]),
          reads=[R_ac16, R_idf], writes=[R_pM])
        O("scalar", _fn(lambda e, o_, i_: e.copy(o_, i_), acr[:, :], pM[0:1, 128:144]),
          reads=[R_pM], writes=[R_acr])
        O("sync", _fn(lambda e, o_, i_: e.dma_start(out=o_, in_=i_),
                      agin_d[2048:2049, 0:16], acr[:, :]),
          reads=[R_acr], writes=[R_agin], dma=True)
        sk.barrier()
        R_ag = Res("ag")
        cc = O("gpsimd", _fn(lambda e: e.collective_compute(
            "AllGather", ALU.bypass, replica_groups=[list(range(8))],
            ins=[agin_d.opt()], outs=[agout_d.opt()])), writes=[R_ag])
        st.close()
        return R_ag

    def phase_a3(R_ag):
        st = contextlib.ExitStack()
        Wsb = sb("Wosb", (128, DCH * D), BF16, st)
        R_W = [Res(f"Wo{i}") for i in range(4)]
        load_big_weight(Wsb, R_W, Wo_in)
        gwt = sb("gwt", (128, DCH * 8), BF16, st); R_gwt = Res("gwt")

        HW3 = 2 * CH
        qdm = sb("qdm", (128, H * HW3), BF16, st); R_qdm = Res("qdm")
        olm = sb("olm", (128, H * HW3), BF16, st); R_olm = Res("olm")
        gfm = sb("gfm", (128, H * HW3), BF16, st); R_gfm = Res("gfm")
        xcs = [sb(f"xc{i}", (128, D), F32, st) for i in range(2)]
        R_xcs = [Res(f"xc{i}") for i in range(2)]
        sinb = sb("sinb", (128, H * 128), BF16, st); R_sinb = Res("sinb")
        cbs = sb("cbs", (128, 16), F32, st); R_cbs = Res("cbs")
        t1_ = [sb(f"t1a{i}", (128, 128), BF16, st) for i in range(2)]
        R_t1_ = [Res(f"t1a{i}") for i in range(2)]
        ogs_ = [sb(f"ogs{i}", (128, D), BF16, st) for i in range(2)]
        R_ogs_ = [Res(f"ogs{i}") for i in range(2)]
        xms_ = [sb(f"xms{i}", (128, D), F32, st) for i in range(2)]
        R_xms_ = [Res(f"xms{i}") for i in range(2)]
        xl2 = sb("xl2", (128, D), BF16, st); R_xl2 = Res("xl2")
        x2s = sb("x2s", (128, DCH * MAC), BF16, st); R_x2s = Res("x2s")
        lns_ = [sb(f"lns3{i}", (128, 12), F32, st) for i in range(2)]
        R_lns_ = [Res(f"lns3{i}") for i in range(2)]
        ctk = sb("ctk", (128, 8), F32, st); R_ctk = Res("ctk")
        rr = sb("rr", (128, 8), F32, st); R_rr = Res("rr")
        crw = sb("crw", (8, MAC), BF16, st); R_crw = Res("crw")

        pC = ps("pC", (128, 512), F32, st); R_pC = Res("pC")
        pA = [ps(f"pA{i}", (128, 512), F32, st) for i in range(4)]
        R_pA = [Res(f"pA{i}") for i in range(4)]
        pT3 = ps("pT3", (128, 1024), BF16, st); R_pT3 = Res("pT3")
        pL = ps("pL", (128, 512), F32, st); R_pL = Res("pL")
        pCt = ps("pCt", (128, 512), F32, st); R_pCt = Res("pCt")

        st2 = contextlib.ExitStack()
        gwb = xl2; R_gwb = R_xl2
        O("gpsimd", _fn(lambda e, o_, i_: e.dma_start(out=o_, in_=i_), gwb[0:8, :], gw_in),
          writes=[R_gwb], dma=True)
        sin = xms_[0]
        sina = xcs[1]
        R_sin = [Res(f"sin{i}") for i in range(2)]
        sld = xcs[0]; R_sld = Res("sld")
        asg = sb("asg", (8, 16), F32, st2); R_asg = Res("asg")
        gam = [sb(f"gam{i}", (8, 16), F32, st2) for i in range(2)]
        R_gam = [Res(f"gam{i}") for i in range(2)]

        for k in range(DCH):
            O("tensor", _fn(lambda e, o_, i_, id_: e.transpose(o_, i_, id_),
                            pT3[0:128, 0:8], gwb[0:8, k * 128:(k + 1) * 128], id_bf[0:8, 0:8]),
              reads=[R_gwb, R_idb], writes=[R_pT3])
            O("vector", _fn(lambda e, o_, i_, kk=k: e.tensor_copy(o_, i_),
                            gwt[:, k * 8:(k + 1) * 8], pT3[0:128, 0:8]),
              reads=[R_pT3], writes=[R_gwt])

        O("sync", _fn(lambda e, o_, i_: e.dma_start(out=o_, in_=i_),
                      asg[:, :], agout_d.rearrange("(c r) v -> c r v", r=2176)[:, 2048, 0:16]),
          writes=[R_asg], extra=[R_ag.w] if R_ag.w else [], dma=True)
        gfac = sb("gfac", (8, 16), F32, st2); R_gfac = Res("gfac")
        O("vector", _fn(lambda e, a: e.memset(a, 1.0), gam[0][:, :]), writes=[R_gam[0]])
        O("vector", _fn(lambda e, o_, i_, s_: e.tensor_scalar(o_, i_, s_, None, ALU.mult),
                        gam[1][:, :], gam[0][:, :], sel8[:, :]),
          reads=[R_sel8, R_gam[0]], writes=[R_gam[1]])
        for cc8 in range(8):
            cur, nxt = (cc8 + 1) % 2, cc8 % 2
            mm(pCt[0:8, 0:16], rowsel8[:, cc8 * 8:(cc8 + 1) * 8], asg[:, :],
               reads=[R_rs8, R_asg], wres=R_pCt, start=True, stop=True)
            O("vector", _fn(lambda e, o_, i_, m_: e.tensor_scalar(o_, i_, 1.0, m_, ALU.subtract, ALU.mult),
                            gfac[:, :], pCt[0:8, 0:16], mask8[:, cc8:cc8 + 1]),
              reads=[R_pCt, R_msk8], writes=[R_gfac])
            O("vector", _fn(lambda e, o_, i_: e.tensor_scalar(o_, i_, 1.0, None, ALU.add),
                            gfac[:, :], gfac[:, :]), reads=[R_gfac], writes=[R_gfac])
            O("vector", _fn(lambda e, o_, a_, b_: e.tensor_tensor(o_, a_, b_, ALU.mult),
                            gam[nxt][:, :], gam[cur][:, :], gfac[:, :]),
              reads=[R_gam[cur], R_gfac], writes=[R_gam[nxt]])
        gam_fin = gam[(8 - 1) % 2]
        R_gam_fin = R_gam[(8 - 1) % 2]
        O("vector", _fn(lambda e, a: e.memset(a, 0.0), sin[:, :]), writes=[R_sin[0]])
        for cp in range(8):
            O("sync", _fn(lambda e, o_, i_: e.dma_start(out=o_, in_=i_),
                          sld[:, :].rearrange("p (h v) -> p h v", v=128),
                          agout_d.rearrange("(c r) v -> c r v", r=2176)[cp, 0:2048, :]
                          .rearrange("(h k) v -> k h v", k=128)),
              writes=[R_sld], extra=[R_ag.w] if R_ag.w else [], dma=True)
            mm(pCt[:, 0:16], rowsel8w[:, cp * 128:(cp + 1) * 128], gam_fin[:, :],
               reads=[R_rs8w, R_gam_fin], wres=R_pCt, start=True, stop=True)
            O("vector", _fn(lambda e, o_, i_: e.tensor_copy(o_, i_), cbs[:, :], pCt[:, 0:16]),
              reads=[R_pCt], writes=[R_cbs])
            src_b, dst_b = (sin, sina) if cp % 2 == 0 else (sina, sin)
            Rs, Rd = R_sin[cp % 2], R_sin[1 - cp % 2]
            for h in range(H):
                O("vector", _fn(lambda e, o_, i_, s_, a_, hh=h: e.scalar_tensor_tensor(o_, i_, s_, a_, ALU.mult, ALU.add),
                                dst_b[:, h * 128:(h + 1) * 128], sld[:, h * 128:(h + 1) * 128],
                                cbs[:, h:h + 1], src_b[:, h * 128:(h + 1) * 128]),
                  reads=[R_sld, R_cbs, Rs], writes=[Rd])
        fin = sin if 8 % 2 == 0 else sina
        R_fin = R_sin[0]
        O("vector", _fn(lambda e, o_, i_: e.tensor_copy(o_, i_), sinb[:, :], fin[:, :]),
          reads=[R_fin], writes=[R_sinb])
        sk.barrier()
        st2.close()

        for m in range(NMAC):
            off = m * MAC
            for c in range(CPM):
                o2 = off + c * CH
                t1, R_t1 = t1_[c % 2], R_t1_[c % 2]
                ogs, R_ogs = ogs_[c % 2], R_ogs_[c % 2]
                xms, R_xms = xms_[c % 2], R_xms_[c % 2]
                lns, R_lns = lns_[c % 2], R_lns_[c % 2]
                if c % 2 == 0:
                    for buf, res, dsrc, eng in ((qdm, R_qdm, qd_d, "sync"), (olm, R_olm, ol_d, "scalar"),
                                                (gfm, R_gfm, gf_d, "sync")):
                        O(eng, _fn(lambda e, o_, i_: e.dma_start(out=o_, in_=i_),
                                   buf[:, :].rearrange("p (d t) -> p d t", t=HW3),
                                   dsrc.rearrange("d p t -> p d t")[:, :, o2:o2 + HW3]),
                          writes=[res], dma=True)
                xc = xcs[c % 2]; R_xc = R_xcs[c % 2]
                O("sync", _fn(lambda e, o_, i_: e.dma_start(out=o_, in_=i_),
                              xc[:, :], x_in[HALO + o2:HALO + o2 + 128, :]),
                  writes=[R_xc], dma=True)
                c3 = c % 2
                pCb_ = [pC, pCt]
                R_pCb_ = [R_pC, R_pCt]

                def corr_mm(h):
                    mm(pCb_[h % 2][:, 0:128], sinb[:, h * 128:(h + 1) * 128],
                       qdm[:, h * HW3 + c3 * CH: h * HW3 + (c3 + 1) * CH],
                       reads=[R_sinb, R_qdm], wres=R_pCb_[h % 2], start=True, stop=True)

                corr_mm(0)
                corr_mm(1)
                for h in range(H):
                    hp = h % 2
                    O("vector", _fn(lambda e, o_, a_, b_: e.tensor_tensor(o_, a_, b_, ALU.add),
                                    t1[:, :], olm[:, h * HW3 + c3 * CH: h * HW3 + (c3 + 1) * CH],
                                    pCb_[hp][:, 0:128]),
                      reads=[R_olm, R_pCb_[hp]], writes=[R_t1])
                    if h + 2 < H:
                        corr_mm(h + 2)
                    O("vector", _fn(lambda e, o_, a_, b_: e.tensor_tensor(o_, a_, b_, ALU.mult),
                                    ogs[:, h * 128:(h + 1) * 128], t1[:, :],
                                    gfm[:, h * HW3 + c3 * CH: h * HW3 + (c3 + 1) * CH]),
                      reads=[R_t1, R_gfm], writes=[R_ogs])
                for dt in range(4):
                    for h in range(H):
                        mm(pA[dt][:, 0:512], ogs[:, h * 128:(h + 1) * 128],
                           Wsb[:, h * D + dt * 512: h * D + (dt + 1) * 512],
                           reads=[R_ogs, R_W[h // 4]], wres=R_pA[dt], start=(h == 0), stop=(h == H - 1))
                    O("vector", _fn(lambda e, o_, a_, b_: e.tensor_tensor(o_, a_, b_, ALU.add),
                                    xms[:, dt * 512:(dt + 1) * 512], pA[dt][:, 0:512],
                                    xc[:, dt * 512:(dt + 1) * 512]),
                      reads=[R_pA[dt], R_xc], writes=[R_xms])
                O("gpsimd", _fn(lambda e, o_, i_: e.dma_start(out=o_, in_=i_),
                                xmid_d[o2:o2 + 128, :], xms[:, :]),
                  reads=[R_xms], dma=True)
                ln_norm(st, R_xms, xms[:, :], xl2[:, :], R_xl2, lns, R_lns, xl2[:, :], R_xl2, D)
                for d in range(DCH):
                    O("tensor", _fn(lambda e, o_, i_, id_: e.transpose(o_, i_, id_),
                                    pT3[:, 0:128], xl2[:, d * 128:(d + 1) * 128], id_bf[:, :]),
                      reads=[R_xl2, R_idb], writes=[R_pT3])
                    eng = "vector" if d % 2 == 0 else "scalar"
                    fncp = (lambda e, o_, i_: e.tensor_copy(o_, i_)) if eng == "vector" else \
                           (lambda e, o_, i_: e.copy(o_, i_))
                    O(eng, _fn(fncp, x2s[:, d * MAC + c * CH: d * MAC + (c + 1) * CH], pT3[:, 0:128]),
                      reads=[R_pT3], writes=[R_x2s])
                for k in range(DCH):
                    mm(pL[:, 0:8], x2s[:, k * MAC + c * CH: k * MAC + (c + 1) * CH],
                       gwt[:, k * 8:(k + 1) * 8],
                       reads=[R_x2s, R_gwt], wres=R_pL, start=(k == 0), stop=(k == DCH - 1))
                m1, nm1, zz, rz, m2 = (lns[:, i:i + 1] for i in range(5, 10))
                O("vector", _fn(lambda e, o_, i_: e.tensor_reduce(o_, i_, AX.X, ALU.max),
                                m1, pL[:, 0:8]), reads=[R_pL], writes=[R_lns])
                O("vector", _fn(lambda e, o_, i_: e.tensor_scalar(o_, i_, -1.0, None, ALU.mult),
                                nm1, m1), reads=[R_lns], writes=[R_lns])
                O("scalar", _fn(lambda e, o_, i_, b_: e.activation(o_, i_, AF.Exp, bias=b_),
                                rr[:, :], pL[:, 0:8], nm1), reads=[R_pL, R_lns], writes=[R_rr])
                O("vector", _fn(lambda e, o_, i_: e.tensor_reduce(o_, i_, AX.X, ALU.add),
                                zz, rr[:, :]), reads=[R_rr], writes=[R_lns])
                O("vector", _fn(lambda e, o_, i_: e.reciprocal(o_, i_), rz, zz),
                  reads=[R_lns], writes=[R_lns])
                O("vector", _fn(lambda e, o_, i_, s_: e.tensor_scalar(o_, i_, s_, None, ALU.is_ge),
                                ctk[:, :], pL[:, 0:8], m1), reads=[R_pL, R_lns], writes=[R_ctk])
                O("vector", _fn(lambda e, o_, i_, a_: e.scalar_tensor_tensor(o_, i_, NEGBIG, a_, ALU.mult, ALU.add),
                                ctk[:, :], ctk[:, :], pL[:, 0:8]),
                  reads=[R_ctk, R_pL], writes=[R_ctk])
                O("vector", _fn(lambda e, o_, i_: e.tensor_reduce(o_, i_, AX.X, ALU.max),
                                m2, ctk[:, :]), reads=[R_ctk], writes=[R_lns])
                O("vector", _fn(lambda e, o_, i_, s_: e.tensor_scalar(o_, i_, s_, None, ALU.is_ge),
                                ctk[:, :], pL[:, 0:8], m2), reads=[R_pL, R_lns], writes=[R_ctk])
                O("vector", _fn(lambda e, o_, i_, s_, a_: e.scalar_tensor_tensor(o_, i_, s_, a_, ALU.mult, ALU.mult),
                                ctk[:, :], rr[:, :], rz, ctk[:, :]),
                  reads=[R_rr, R_lns, R_ctk], writes=[R_ctk])
                O("tensor", _fn(lambda e, o_, i_, id_: e.transpose(o_, i_, id_),
                                pCt[0:8, 0:128], ctk[:, :], id_f[:, :]),
                  reads=[R_ctk, R_idf], writes=[R_pCt])
                O("vector", _fn(lambda e, o_, i_: e.tensor_copy(o_, i_),
                                crw[:, c * CH:(c + 1) * CH], pCt[0:8, 0:128]),
                  reads=[R_pCt], writes=[R_crw])
            O("gpsimd", _fn(lambda e, o_, i_: e.dma_start(out=o_, in_=i_),
                            crow_d[:, off:off + MAC], crw[:, :]), reads=[R_crw], dma=True)
            O("gpsimd", _fn(lambda e, o_, i_: e.dma_start(out=o_, in_=i_),
                            xln2_d.rearrange("d p t -> p d t")[:, :, off:off + MAC],
                            x2s[:, :].rearrange("p (d t) -> p d t", t=MAC)),
              reads=[R_x2s], dma=True)
        sk.barrier()
        st.close()

    def phase_f1():
        st = contextlib.ExitStack()
        w1t = [sb(f"w1t{i}", (128, DCH * INTER), BF16, st) for i in range(2)]
        R_w1t = [[Res(f"w1t{i}_{d}") for d in range(DCH)] for i in range(2)]
        w3t = sb("w3t", (128, DCH * INTER), BF16, st)
        R_w3t = [Res(f"w3t{d}") for d in range(DCH)]
        x2m_ = [sb(f"x2m{i}", (128, DCH * MAC), BF16, st) for i in range(4)]
        R_x2m_ = [Res(f"x2m{i}") for i in range(4)]
        cbc = [sb(f"cbc{i}", (128, MAC), F32, st) for i in range(2)]
        R_cbc = [Res(f"cbc{i}") for i in range(2)]
        crl = sb("crl", (8, T), F32, st); R_crl = Res("crl")
        sil = [sb(f"sil{i}", (128, MAC), F32, st) for i in range(2)]
        R_sil = [Res(f"sil{i}") for i in range(2)]
        g0 = [sb(f"g0{i}", (128, MAC), F32, st) for i in range(2)]
        R_g0 = [Res(f"g0{i}") for i in range(2)]
        gp = [sb(f"gp{i}", (128, MAC), BF16, st) for i in range(4)]
        R_gp = [Res(f"gp{i}") for i in range(4)]
        pH = [ps(f"pH{i}", (128, 512), F32, st) for i in range(4)]
        R_pH = [Res(f"pH{i}") for i in range(4)]
        pH3 = [ps(f"pH3{i}", (128, 512), F32, st) for i in range(4)]
        R_pH3 = [Res(f"pH3{i}") for i in range(4)]

        O("sync", _fn(lambda e, o_, i_: e.dma_start(out=o_, in_=i_), crl[:, :], crow_d),
          writes=[R_crl], dma=True)

        def load_w1(e_, slot):
            for d in range(DCH):
                O("scalar", _fn(lambda e, o_, i_: e.dma_start(out=o_, in_=i_, transpose=True),
                                w1t[slot][:, d * INTER:(d + 1) * INTER],
                                w1b_d[e_][:, d * 128:(d + 1) * 128]),
                  reads=[R_w1b[e_]], writes=[R_w1t[slot][d]], dma=True)

        def load_w3(e_):
            for d in range(DCH):
                O("scalar", _fn(lambda e, o_, i_: e.dma_start(out=o_, in_=i_, transpose=True),
                                w3t[:, d * INTER:(d + 1) * INTER],
                                w3b_d[e_][:, d * 128:(d + 1) * 128]),
                  reads=[R_w3b[e_]], writes=[R_w3t[d]], dma=True)

        load_w1(0, 0)
        load_w3(0)
        NPAIR = NMAC // 2
        for e_ in range(NE):
            slot = e_ % 2
            if e_ + 1 < NE:
                load_w1(e_ + 1, 1 - slot)
            for mp in range(NPAIR):
                x2a, x2b = x2m_[2 * (mp % 2)], x2m_[2 * (mp % 2) + 1]
                Rxa, Rxb = R_x2m_[2 * (mp % 2)], R_x2m_[2 * (mp % 2) + 1]
                offs = (mp * 2 * MAC, (mp * 2 + 1) * MAC)
                for mi, (xb, Rxb_) in enumerate(((x2a, Rxa), (x2b, Rxb))):
                    off = offs[mi]
                    O("sync", _fn(lambda e, o_, i_: e.dma_start(out=o_, in_=i_),
                                  xb[:, :].rearrange("p (d t) -> p d t", t=MAC),
                                  xln2_d.rearrange("d p t -> p d t")[:, :, off:off + MAC]),
                      writes=[Rxb_], dma=True)
                    if e_ < E:
                        mm(pH[2 + mi][:, 0:MAC], rowsel8w[:, e_ * 128:(e_ + 1) * 128],
                           crl[:, off:off + MAC],
                           reads=[R_rs8w, R_crl], wres=R_pH[2 + mi], start=True, stop=True)
                        O("vector", _fn(lambda e, o_, i_: e.tensor_copy(o_, i_),
                                        cbc[mi][:, :], pH[2 + mi][:, 0:MAC]),
                          reads=[R_pH[2 + mi]], writes=[R_cbc[mi]])
                for ic in range(ICH):
                    par = ic % 2
                    phs = (pH[2 * par], pH[2 * par + 1])
                    Rphs = (R_pH[2 * par], R_pH[2 * par + 1])
                    ph3s = (pH3[2 * par], pH3[2 * par + 1])
                    Rph3s = (R_pH3[2 * par], R_pH3[2 * par + 1])
                    for k in range(DCH):
                        lt = w1t[slot][:, k * INTER + ic * 128: k * INTER + (ic + 1) * 128]
                        mm(phs[0][:, 0:MAC], lt, x2a[:, k * MAC:(k + 1) * MAC],
                           reads=[R_w1t[slot][k], Rxa], wres=Rphs[0], start=(k == 0), stop=(k == DCH - 1))
                        mm(phs[1][:, 0:MAC], lt, x2b[:, k * MAC:(k + 1) * MAC],
                           reads=[R_w1t[slot][k], Rxb], wres=Rphs[1], start=(k == 0), stop=(k == DCH - 1))
                    for k in range(DCH):
                        lt = w3t[:, k * INTER + ic * 128: k * INTER + (ic + 1) * 128]
                        mm(ph3s[0][:, 0:MAC], lt, x2a[:, k * MAC:(k + 1) * MAC],
                           reads=[R_w3t[k], Rxa], wres=Rph3s[0], start=(k == 0), stop=(k == DCH - 1))
                        mm(ph3s[1][:, 0:MAC], lt, x2b[:, k * MAC:(k + 1) * MAC],
                           reads=[R_w3t[k], Rxb], wres=Rph3s[1], start=(k == 0), stop=(k == DCH - 1))
                    for mi in range(2):
                        off = offs[mi]
                        O("scalar", _fn(lambda e, o_, i_: e.activation(o_, i_, AF.Silu),
                                        sil[mi][:, :], phs[mi][:, 0:MAC]),
                          reads=[Rphs[mi]], writes=[R_sil[mi]])
                        gpb = gp[2 * par + mi]; R_gpb = R_gp[2 * par + mi]
                        if e_ < E:
                            O("vector", _fn(lambda e, o_, a_, b_: e.tensor_tensor(o_, a_, b_, ALU.mult),
                                            g0[mi][:, :], sil[mi][:, :], ph3s[mi][:, 0:MAC]),
                              reads=[R_sil[mi], Rph3s[mi]], writes=[R_g0[mi]])
                            O("vector", _fn(lambda e, o_, a_, b_: e.tensor_tensor(o_, a_, b_, ALU.mult),
                                            gpb[:, :], g0[mi][:, :], cbc[mi][:, :]),
                              reads=[R_g0[mi], R_cbc[mi]], writes=[R_gpb])
                        else:
                            O("vector", _fn(lambda e, o_, a_, b_: e.tensor_tensor(o_, a_, b_, ALU.mult),
                                            gpb[:, :], sil[mi][:, :], ph3s[mi][:, 0:MAC]),
                              reads=[R_sil[mi], Rph3s[mi]], writes=[R_gpb])
                        O("gpsimd", _fn(lambda e, o_, i_: e.dma_start(out=o_, in_=i_),
                                        g3_d[e_, off // CH:off // CH + CPM, :, ic, :]
                                        .rearrange("s p t -> p s t"),
                                        gpb[:, :].rearrange("p (s t) -> p s t", t=CH)),
                          reads=[R_gpb], dma=True)
            if e_ + 1 < NE:
                load_w3(e_ + 1)
        sk.barrier()
        st.close()

    R_g3 = [[Res(f"g3_{e}_{i}") for i in range(ICH)] for e in range(NE)]

    def phase_f2():
        st = contextlib.ExitStack()
        w2t = sb("w2t", (128, NE * ICH * 1024), BF16, st)
        R_w2t = [Res(f"w2t{i}") for i in range(NE * ICH)]
        gls = [sb(f"gls{i}", (128, NE * ICH * 128), BF16, st) for i in range(2)]
        R_gls = [[Res(f"gls{i}_{e}") for e in range(NE)] for i in range(2)]
        xmb = [sb(f"xmb{i}", (128, 1024), F32, st) for i in range(2)]
        R_xmb = [Res(f"xmb{i}") for i in range(2)]
        ob = [sb(f"ob{i}", (128, 1024), F32, st) for i in range(2)]
        R_ob = [Res(f"ob{i}") for i in range(2)]
        pY = [ps(f"pY{i}", (128, 512), F32, st) for i in range(4)]
        R_pY = [Res(f"pY{i}") for i in range(4)]

        for half in range(2):
            d0 = half * 1024
            for e_ in range(NE):
                for ic in range(ICH):
                    O("scalar", _fn(lambda e, o_, i_: e.dma_start(out=o_, in_=i_, transpose=True),
                                    w2t[:, (e_ * ICH + ic) * 1024:(e_ * ICH + ic + 1) * 1024],
                                    w2b_d[e_][d0:d0 + 1024, ic * 128:(ic + 1) * 128]),
                      reads=[R_w2b[e_]], writes=[R_w2t[e_ * ICH + ic]], dma=True)
            for ts_ in range(NCH):
                o2 = ts_ * CH
                gb = gls[ts_ % 2]; R_gb = R_gls[ts_ % 2]
                for e_ in range(NE):
                    O("sync" if e_ % 2 == 0 else "gpsimd",
                      _fn(lambda e, o_, i_: e.dma_start(out=o_, in_=i_),
                          gb[:, e_ * ICH * 128:(e_ + 1) * ICH * 128],
                          g3_d[e_, ts_]),
                      writes=[R_gb[e_]], dma=True)
                xm = xmb[ts_ % 2]; R_xm = R_xmb[ts_ % 2]
                O("sync", _fn(lambda e, o_, i_: e.dma_start(out=o_, in_=i_),
                              xm[:, :], xmid_d[o2:o2 + 128, d0:d0 + 1024]),
                  writes=[R_xm], dma=True)
                pa, pb_ = pY[2 * (ts_ % 2)], pY[2 * (ts_ % 2) + 1]
                Rpa, Rpb = R_pY[2 * (ts_ % 2)], R_pY[2 * (ts_ % 2) + 1]
                n_mm = NE * ICH
                idx = 0
                for e_ in range(NE):
                    for ic in range(ICH):
                        lt = gb[:, (e_ * ICH + ic) * 128:(e_ * ICH + ic + 1) * 128]
                        mm(pa[:, 0:512], lt,
                           w2t[:, (e_ * ICH + ic) * 1024:(e_ * ICH + ic) * 1024 + 512],
                           reads=[R_gb[e_], R_w2t[e_ * ICH + ic]], wres=Rpa,
                           start=(idx == 0), stop=(idx == n_mm - 1))
                        mm(pb_[:, 0:512], lt,
                           w2t[:, (e_ * ICH + ic) * 1024 + 512:(e_ * ICH + ic + 1) * 1024],
                           reads=[R_gb[e_], R_w2t[e_ * ICH + ic]], wres=Rpb,
                           start=(idx == 0), stop=(idx == n_mm - 1))
                        idx += 1
                obf = ob[ts_ % 2]; R_obf = R_ob[ts_ % 2]
                O("vector", _fn(lambda e, o_, a_, b_: e.tensor_tensor(o_, a_, b_, ALU.add),
                                obf[:, 0:512], pa[:, 0:512], xm[:, 0:512]),
                  reads=[Rpa, R_xm], writes=[R_obf])
                O("vector", _fn(lambda e, o_, a_, b_: e.tensor_tensor(o_, a_, b_, ALU.add),
                                obf[:, 512:1024], pb_[:, 0:512], xm[:, 512:1024]),
                  reads=[Rpb, R_xm], writes=[R_obf])
                O("gpsimd", _fn(lambda e, o_, i_: e.dma_start(out=o_, in_=i_),
                                y_out[o2:o2 + 128, d0:d0 + 1024], obf[:, :]),
                  reads=[R_obf], dma=True)
        st.close()


    phase_a1_qk("q")
    phase_a1_qk("k")
    phase_a1g()
    R_ag = phase_a2()
    phase_a3(R_ag)
    ast.close()
    phase_f1()
    sk.barrier()
    phase_f2()
    sk.finalize()
    top.close()
    return nc




def _consts(core):
    b, r = core // 4, core % 4
    eye = np.eye(128, dtype=np.float32)
    selcol = np.zeros((128, 16 * H), dtype=BF)
    for c in range(H):
        selcol[:, c * 16 + c] = 1
    rowsel = np.zeros((16, 128 * H), dtype=np.float32)
    for c in range(H):
        rowsel[c, c * 128:(c + 1) * 128] = 1
    rs8 = np.zeros((8, 64), dtype=np.float32)
    for c in range(8):
        rs8[c, c * 8:(c + 1) * 8] = 1
    rs8w = np.zeros((8, 128 * 8), dtype=np.float32)
    for c in range(8):
        rs8w[c, c * 128:(c + 1) * 128] = 1
    s, t = np.arange(128)[:, None], np.arange(128)[None, :]
    maskbig = np.where(s <= t, 0.0, NEGBIG).astype(np.float32)
    sel8 = np.zeros((8, 1), dtype=np.float32)
    mask8 = np.zeros((8, 8), dtype=np.float32)
    for cp in range(8):
        bp, rp = cp // 4, cp % 4
        if bp == b and rp < r:
            sel8[cp, 0] = 1
            for cm in range(8):
                bm, rm = cm // 4, cm % 4
                if bm == b and rp < rm < r:
                    mask8[cp, cm] = 1
    return {
        "c_rowsel_bf": rowsel.astype(BF),
        "c_id_bf": eye.astype(BF), "c_id_f": eye,
        "c_ones": np.ones((1, 128), np.float32),
        "c_maskbig": maskbig, "c_selcol": selcol, "c_rowsel": rowsel,
        "c_rowsel8": rs8, "c_rowsel8w": rs8w, "c_sel8": sel8, "c_mask8": mask8,
    }


def kernel(**inputs):
    x = np.asarray(inputs["L0_attn_hidden_states"], np.float32)
    nm = {
        "Wq": "L0_attn_Wq", "Wk": "L0_attn_Wk", "Wv": "L0_attn_Wv", "Wg": "L0_attn_Wg",
        "Wo": "L0_attn_Wo", "Wb": "L0_attn_Wb", "bb": "L0_attn_bb",
        "qcw": "L0_attn_qconv_w", "qcb": "L0_attn_qconv_b",
        "kcw": "L0_attn_kconv_w", "kcb": "L0_attn_kconv_b",
        "qnw": "L0_attn_qn_w", "qnb": "L0_attn_qn_b",
        "knw": "L0_attn_kn_w", "knb": "L0_attn_kn_b",
        "gate_w": "L0_ffn_gate_w", "w1": "L0_ffn_w1", "w2": "L0_ffn_w2", "w3": "L0_ffn_w3",
        "ws1": "L0_ffn_ws1", "ws2": "L0_ffn_ws2", "ws3": "L0_ffn_ws3",
    }
    weights = {k: np.ascontiguousarray(np.asarray(inputs[v], np.float32)) for k, v in nm.items()}
    nc = build_nc()
    in_maps = []
    for c in range(NCORE):
        b, r = c // 4, c % 4
        if r == 0:
            shard = np.concatenate([np.zeros((HALO, D), np.float32), x[b, 0:T]], 0)
        else:
            shard = x[b, r * T - HALO: (r + 1) * T]
        d = {"x": np.ascontiguousarray(shard)}
        d.update(weights)
        d.update(_consts(c))
        in_maps.append(d)
    res = run_bass_kernel_spmd(nc, in_maps, core_ids=list(range(NCORE)))
    out = np.empty((B, L, D), np.float32)
    for c in range(NCORE):
        b, r = c // 4, c % 4
        out[b, r * T:(r + 1) * T] = res.results[c]["y"]
    return out
